# version 04
# baseline (speedup 1.0000x reference)
import contextlib
import numpy as np
import ml_dtypes
import concourse.bass as bass
import concourse.mybir as mybir
from concourse.bass_isa import ReduceOp
from concourse.library_config import attnmlp
from concourse.bass_utils import run_bass_kernel_spmd

F32 = mybir.dt.float32
I32 = mybir.dt.int32
BF16 = mybir.dt.bfloat16
AF = mybir.ActivationFunctionType
ALU = mybir.AluOpType
AXX = mybir.AxisListType.X


def _split2(v):
    hi = np.float32(v)
    return float(hi), float(np.float32(v - np.float64(hi)))


C_ROPE_HI, C_ROPE_LO = _split2(-np.log(np.float64(10000.0)) / 32.0)
PI2_HI, PI2_LO = _split2(2.0 * np.pi)
PIH_HI, PIH_LO = _split2(0.5 * np.pi)
INV_2PI = float(np.float32(1.0 / (2.0 * np.pi)))
R127 = float(np.float32(1.0 / 127.0))
PI_F = float(np.float32(np.pi))

SHUF_MASK = []
for _i in range(0, 32, 2):
    SHUF_MASK += [_i + 1, _i]


class Plan:
    ENGS = ("sp", "act", "dve", "pe", "gp")

    def __init__(self):
        self.ops = {e: [] for e in self.ENGS}
        self.C = {}
        self.prod = {}
        self.cons = {}
        self.pending = {}

    def mark(self, sem):
        return self.C.get(sem, 0)

    skip = ()

    def op(self, eng, fns, reads=(), writes=(), sem=None, dk=1, waits=(), tag=""):
        if any(tag.startswith(p) for p in self.skip):
            return None
        if not isinstance(fns, (list, tuple)):
            fns = [fns]
        wl = {}

        def addw(w):
            if w is None:
                return
            if isinstance(w, dict):
                for sv in w.items():
                    addw(sv)
                return
            s, v = w
            if v > 0:
                wl[s] = max(wl.get(s, 0), v)

        for r in reads:
            addw(self.prod.get(r))
        for r in writes:
            addw(self.prod.get(r))
            addw(self.cons.get(r))
            for w in self.pending.pop(r, ()):
                addw(w)
        for w in waits:
            addw(w)
        assert sem is not None, f"op needs sem ({tag})"
        self.C[sem] = self.C.get(sem, 0) + dk
        after = (sem, self.C[sem])
        self.ops[eng].append((tuple(wl.items()), list(fns), (sem, dk), tag))
        for r in reads:
            self.cons.setdefault(r, {})[sem] = self.C[sem]
        for r in writes:
            self.prod[r] = {sem: self.C[sem]}
            self.cons[r] = {}
        return after


class Arena:

    def __init__(self, handle, ncols):
        self.h = handle
        self.ncols = ncols
        self.live = []

    def take(self, P, c0, c1, name):
        assert 0 <= c0 < c1 <= self.ncols, (c0, c1, self.ncols, name)
        keep = []
        for a0, a1, nm in self.live:
            if a1 > c0 and a0 < c1:
                if nm != name:
                    for w in (P.prod.get(nm), P.cons.get(nm)):
                        if w is not None:
                            P.pending.setdefault(name, []).append(w)
                    if a0 < c0:
                        keep.append((a0, c0, nm))
                    if a1 > c1:
                        keep.append((c1, a1, nm))
            else:
                keep.append((a0, a1, nm))
        self.live = keep + [(c0, c1, name)]
        return self.h[:, c0:c1]


def build(cfg):
    T, D, H, I, NB = cfg["T"], cfg["D"], cfg["H"], cfg["I"], cfg["NB"]
    NCORES = cfg["ncores"]
    PAIRS = H // 2
    ND = D // 128
    NI = I // 128
    TOK = NB * T
    CH = min(512, T)
    NCH_B = T // CH
    QCH = min(512, T)
    NQC = T // QCH
    KT = T // 128
    VW = min(512, D)
    NVC = D // VW
    HW2 = I // 2

    nc = bass.Bass("TRN2", target_bir_lowering=False, debug=False)

    xT_h = nc.dram_tensor("xT", [D, TOK], F32, kind="ExternalInput").ap()
    wqkvT_h = nc.dram_tensor("wqkvT", [D, 3 * D], F32, kind="ExternalInput").ap()
    bqkv_h = nc.dram_tensor("bqkv", [1, 3 * D], F32, kind="ExternalInput").ap()
    woutT_h = nc.dram_tensor("woutT", [D, D], F32, kind="ExternalInput").ap()
    bout_h = nc.dram_tensor("bout", [1, D], F32, kind="ExternalInput").ap()
    clipx_h = nc.dram_tensor("clip_x", [1, 1], F32, kind="ExternalInput").ap()
    clipd_h = nc.dram_tensor("clip_down", [1, 1], F32, kind="ExternalInput").ap()
    wg_h = nc.dram_tensor("wg", [I, D], F32, kind="ExternalInput").ap()
    wgT_h = nc.dram_tensor("wgT", [D, I], F32, kind="ExternalInput").ap()
    wu_h = nc.dram_tensor("wu", [I, D], F32, kind="ExternalInput").ap()
    wuT_h = nc.dram_tensor("wuT", [D, I], F32, kind="ExternalInput").ap()
    wd_h = nc.dram_tensor("wd", [D, I], F32, kind="ExternalInput").ap()
    wdT_h = nc.dram_tensor("wdT", [I, D], F32, kind="ExternalInput").ap()
    outT_h = nc.dram_tensor("outT", [D, TOK], F32, kind="ExternalOutput").ap()

    qt_hbm = nc.dram_tensor("qt_s", [NB * PAIRS, 128, T], BF16).ap()
    kt_hbm = nc.dram_tensor("kt_s", [NB * PAIRS, 128, T], BF16).ap()
    au_hbm = nc.dram_tensor("au_s", [D, T], BF16).ap()
    ln2_hbm = nc.dram_tensor("ln2_s", [D, TOK], BF16).ap()
    x1_hbm = nc.dram_tensor("x1_s", [D, TOK], BF16).ap()
    h_hbm = nc.dram_tensor("h_s", [I, TOK], BF16).ap()
    wgq_hbm = nc.dram_tensor("wgq_s", [D, I], BF16).ap()
    wuq_hbm = nc.dram_tensor("wuq_s", [D, I], BF16).ap()
    wdq_hbm = nc.dram_tensor("wdq_s", [I, D], BF16).ap()
    sg_hbm = nc.dram_tensor("sg_s", [1, I], F32).ap()
    su_hbm = nc.dram_tensor("su_s", [1, I], F32).ap()
    sd_hbm = nc.dram_tensor("sd_s", [1, D], F32).ap()
    cc1i = nc.dram_tensor("cc1i", [1, 8], F32).ap()
    cc1o = nc.dram_tensor("cc1o", [1, 8], F32).ap()
    cc2i = nc.dram_tensor("cc2i", [1, 8], F32).ap()
    cc2o = nc.dram_tensor("cc2o", [1, 8], F32).ap()

    es = contextlib.ExitStack()
    P = Plan()
    P.skip = tuple(cfg.get("skip", ()))

    def sb(name, p, w, dt):
        return es.enter_context(nc.sbuf_tensor(name, [p, w], dt))

    with es:
        cosP = sb("cosP", 128, T, BF16)
        sinP = sb("sinP", 128, T, BF16)
        pidx = sb("pidx", 128, 1, F32)
        cells = sb("cells", 128, 16, F32)
        onesb = sb("onesb", 128, 1, BF16)
        onesr = sb("onesr", 1, CH, BF16)
        wext = sb("wext", 2, 3 * D, BF16)
        woext = sb("woext", 1, D, BF16)
        rowbuf = sb("rowbuf", 1, 3 * CH, F32)
        xext = sb("xext", 2, TOK, BF16)
        maxcols = sb("maxcols", 128, ND * NCH_B * NB, F32)
        hmx = sb("hmx", 128, NI * max(1, TOK // 512), F32)
        sgcols = sb("sgcols", 128, NI, F32)
        sucols = sb("sucols", 128, NI, F32)
        sdcols = sb("sdcols", 128, ND, F32)
        sgq = sb("sgq", 128, NI, F32)
        suq = sb("suq", 128, NI, F32)
        sdq = sb("sdq", 128, ND, F32)
        i32s = sb("i32s", 128, 1024, I32)
        xsq = sb("xsq", 128, CH, BF16)
        bcA = sb("bcA", 128, CH, F32)
        bcB = sb("bcB", 128, CH, F32)
        f32a = sb("f32a", 128, CH, F32)
        f32b = sb("f32b", 128, CH, F32)
        ln2t = sb("ln2t", 128, CH, BF16)
        austg = sb("austg", 64, 2 * QCH, BF16)
        scr1 = sb("scr1", 1, 8, F32)
        hbf = sb("hbf", 128, 512, BF16)
        hqt = [sb(f"hqt{i}", 128, 512, BF16) for i in range(2)]
        hrs = [sb(f"hrs{i}", 128, 512, BF16) for i in range(2)]
        WQW = ND * 3 * D
        AWN = max(ND * TOK, WQW + ND * D, NI * D + 1024)
        arW_h = sb("arW", 128, AWN, BF16)
        arW = Arena(arW_h, AWN)
        AFN = max(2 * ND * CH, 8448, 2 * TOK)
        arF_h = sb("arF", 128, AFN, F32)
        arF = Arena(arF_h, AFN)
        ABN = max(2 * ND * CH, TOK + ND * 512)
        arB_h = sb("arB", 128, ABN, BF16)
        arB = Arena(arB_h, ABN)
        arR_h = sb("arR", 128, 8 * CH, BF16)
        arR = Arena(arR_h, 8 * CH)
        vaug_h = sb("vaug", 128, KT * 65 * H, BF16)
        vaug = Arena(vaug_h, KT * 65 * H)
        pA = es.enter_context(nc.psum_tensor("pA", [128, 1024], F32))
        pB = es.enter_context(nc.psum_tensor("pB", [128, 1024], F32))
        pC = es.enter_context(nc.psum_tensor("pC", [128, 1024], F32))
        pD = es.enter_context(nc.psum_tensor("pD", [128, 1024], F32))
        PS = {"pA0": (pA, 0), "pA1": (pA, 512), "pB0": (pB, 0), "pB1": (pB, 512),
              "pC0": (pC, 0), "pC1": (pC, 512), "pD0": (pD, 0), "pD1": (pD, 512)}
        PSALL = list(PS.keys())

        def dma(eng, qsem, out, in_, reads=(), writes=(), waits=(), tag="dma"):
            return P.op(eng, lambda e, o=out, i=in_: e.dma_start(out=o, in_=i),
                        reads=reads, writes=writes, sem=qsem, dk=16, waits=waits, tag=tag)

        rr = [0]

        def pslot():
            k = ["pA0", "pA1", "pB0", "pB1"][rr[0] % 4]
            rr[0] += 1
            return k, PS[k]

        def load_wqkv():
            w = arW.take(P, 0, WQW, "wqbf")
            for d in range(ND):
                dma("gp", "g_wq", w[:, d * 3 * D:(d + 1) * 3 * D],
                    wqkvT_h[d * 128:(d + 1) * 128, :], writes=["wqbf"], tag="wq ld")
            return w

        tw1 = arF.take(P, 0, T, "tw1")
        tw2 = arF.take(P, T, 2 * T, "tw2")
        tw3 = arF.take(P, 2 * T, 3 * T, "tw3")
        P.op("gp", lambda e: e.iota(pidx[:], pattern=[[0, 1]], channel_multiplier=1,
                                    allow_small_or_imprecise_dtypes=True),
             writes=["pidx"], sem="s_gp", tag="iota p")
        P.op("gp", lambda e: e.iota(tw1[:, 0:T], pattern=[[1, T]], channel_multiplier=0,
                                    allow_small_or_imprecise_dtypes=True),
             writes=["tw1"], sem="s_gp", tag="iota t")
        wqbf = load_wqkv()
        dma("gp", "q_row", wext[1:2, 0:3 * D], bqkv_h[:, :], writes=["wext"])
        dma("gp", "q_row", woext[0:1, 0:D], bout_h[:, :], writes=["woext"])
        dma("gp", "q_row", cells[0:1, 12:13], clipx_h[:, :], writes=["rows0"])
        dma("gp", "q_row", cells[0:1, 5:6], clipd_h[:, :], writes=["rows0"])
        _qrm = {"q_row": P.mark("q_row")}
        for _r in ("wext", "woext", "rows0"):
            P.prod[_r] = dict(_qrm)
        woutT_bf = arW.take(P, WQW, WQW + ND * D, "woutbf")
        for d in range(ND):
            dma("gp", "g_wo", woutT_bf[:, d * D:(d + 1) * D],
                woutT_h[d * 128:(d + 1) * 128, :], writes=["woutbf"], tag="wo ld")
        P.op("gp", [lambda e: e.load_library(attnmlp), lambda e: e.nop()],
             waits=[("g_wq", P.mark("g_wq")), ("q_row", P.mark("q_row")),
                    ("g_wo", P.mark("g_wo"))],
             sem="s_gp", tag="lib")

        P.op("dve", lambda e: e.memset(cells[:, 0:1], 1e-5), writes=["eps"], sem="s_dve")
        P.op("dve", [lambda e: e.tensor_scalar(cells[:, 6:7], pidx[:], 64.0, None, ALU.is_ge),
                     None,
                     lambda e: e.tensor_scalar(cells[:, 6:7], cells[:, 6:7], -64.0, None, ALU.mult),
                     None,
                     lambda e: e.tensor_tensor(cells[:, 6:7], pidx[:], cells[:, 6:7], ALU.add),
                     None,
                     lambda e: e.tensor_scalar(i32s[:, 0:1], cells[:, 6:7], 0.5, -0.25, ALU.mult, ALU.add),
                     None,
                     lambda e: e.tensor_scalar(cells[:, 7:8], i32s[:, 0:1], C_ROPE_HI, None, ALU.mult),
                     lambda e: e.tensor_scalar(cells[:, 8:9], i32s[:, 0:1], C_ROPE_LO, 1.0, ALU.mult, ALU.add)],
             reads=["pidx"], writes=["jc", "i32s"], sem="s_dve", tag="freqj")
        P.op("act", lambda e: e.activation(cells[:, 9:10], cells[:, 7:8], AF.Exp),
             reads=["jc"], writes=["invfe"], sem="s_act", tag="expinvf")
        P.op("dve", lambda e: e.tensor_tensor(cells[:, 9:10], cells[:, 9:10], cells[:, 8:9], ALU.mult),
             reads=["invfe", "jc"], writes=["invf"], sem="s_dve", tag="invf")
        P.op("dve", [lambda e: e.tensor_scalar(i32s[:, 1:2], pidx[:], 0.5, -0.25, ALU.mult, ALU.add),
                     None,
                     lambda e: e.tensor_scalar(cells[:, 10:11], i32s[:, 1:2], -4.0, None, ALU.mult),
                     None,
                     lambda e: e.scalar_tensor_tensor(cells[:, 10:11], pidx[:], 2.0, cells[:, 10:11], ALU.mult, ALU.add),
                     None,
                     lambda e: e.tensor_scalar(cells[:, 10:11], cells[:, 10:11], -1.0, None, ALU.add)],
             reads=["pidx"], writes=["sgn", "i32s"], sem="s_dve", tag="sgn")
        P.op("dve", lambda e: e.tensor_scalar(tw2[:, 0:T], tw1[:, 0:T], cells[:, 9:10], None, ALU.mult),
             reads=["tw1", "invf"], writes=["tw2"], sem="s_dve", tag="ang")
        for h2 in range(max(1, T // 1024)):
            W2 = min(T, 1024)
            a2, b2 = h2 * W2, (h2 + 1) * W2
            P.op("dve", [lambda e, a2=a2, b2=b2, W2=W2: e.tensor_scalar(i32s[:, 0:W2], tw2[:, a2:b2], INV_2PI, None, ALU.mult),
                         None,
                         lambda e, a2=a2, b2=b2, W2=W2: e.tensor_scalar(tw1[:, a2:b2], i32s[:, 0:W2], PI2_HI, None, ALU.mult),
                         None,
                         lambda e, a2=a2, b2=b2: e.tensor_tensor(tw1[:, a2:b2], tw2[:, a2:b2], tw1[:, a2:b2], ALU.subtract),
                         None,
                         lambda e, a2=a2, b2=b2, W2=W2: e.tensor_scalar(tw3[:, a2:b2], i32s[:, 0:W2], PI2_LO, None, ALU.mult),
                         None,
                         lambda e, a2=a2, b2=b2: e.tensor_tensor(tw1[:, a2:b2], tw1[:, a2:b2], tw3[:, a2:b2], ALU.subtract),
                         None,
                         lambda e, a2=a2, b2=b2: e.tensor_scalar(tw3[:, a2:b2], tw1[:, a2:b2], PI_F, None, ALU.is_gt),
                         None,
                         lambda e, a2=a2, b2=b2: e.tensor_scalar(tw3[:, a2:b2], tw3[:, a2:b2], PI2_HI, None, ALU.mult),
                         None,
                         lambda e, a2=a2, b2=b2: e.tensor_tensor(tw1[:, a2:b2], tw1[:, a2:b2], tw3[:, a2:b2], ALU.subtract)],
                 reads=["tw2", "tw1"], writes=["tw1", "tw3", "i32s"], sem="s_dve", tag="nred")
        P.op("dve", [lambda e: e.tensor_scalar(tw2[:, 0:T], tw1[:, 0:T], PIH_HI, PIH_LO, ALU.add, ALU.add),
                     None,
                     lambda e: e.tensor_scalar(tw3[:, 0:T], tw2[:, 0:T], PI_F, None, ALU.is_gt),
                     None,
                     lambda e: e.tensor_scalar(tw3[:, 0:T], tw3[:, 0:T], PI2_HI, None, ALU.mult),
                     None,
                     lambda e: e.tensor_tensor(tw2[:, 0:T], tw2[:, 0:T], tw3[:, 0:T], ALU.subtract)],
             reads=["tw1"], writes=["tw2", "tw3"], sem="s_dve", tag="cosarg")
        P.op("act", lambda e: e.activation(tw3[:, 0:T], tw2[:, 0:T], AF.Sin),
             reads=["tw2"], writes=["tw3"], sem="s_act", tag="sincos")
        P.op("act", lambda e: e.activation(tw2[:, 0:T], tw1[:, 0:T], AF.Sin),
             reads=["tw1", "tw2"], writes=["tw2"], sem="s_act", tag="sinred")
        P.op("dve", lambda e: e.tensor_copy(cosP[:, 0:T], tw3[:, 0:T]),
             reads=["tw3"], writes=["cosP"], sem="s_dve", tag="cosP")
        P.op("dve", lambda e: e.tensor_scalar(sinP[:, 0:T], tw2[:, 0:T], cells[:, 10:11], None, ALU.mult),
             reads=["tw2", "sgn"], writes=["sinP"], sem="s_dve", tag="sinP")

        P.op("dve", lambda e: e.memset(onesb[:], 1.0), writes=["onesb"], sem="s_dve")
        P.op("dve", lambda e: e.memset(onesr[:], 1.0), writes=["onesr"], sem="s_dve")
        for c in range((3 * D) // 512):
            key, (ps, off) = pslot()
            for d in range(ND):
                P.op("pe", lambda e, c=c, d=d, ps=ps, off=off: e.matmul(
                    ps[0:1, off:off + 512], onesb[:],
                    wqbf[:, d * 3 * D + c * 512:d * 3 * D + c * 512 + 512],
                    start=(d == 0), stop=(d == ND - 1)),
                    reads=["onesb", "wqbf"], writes=([key] if d in (0, ND - 1) else ()),
                    sem="s_pe", tag="rsA")
            P.op("dve", lambda e, c=c, ps=ps, off=off: e.tensor_scalar(
                wext[0:1, c * 512:(c + 1) * 512], ps[0:1, off:off + 512], -1.0, None, ALU.mult),
                reads=[key], writes=["wext"], sem="s_dve", tag="rsAe")


        def ln_chunk(b, c, xb, xbb, lnl, lnname, xname, xbname, tag):
            tok0 = b * T + c * CH
            for d in range(ND):
                P.op("act", lambda e, d=d: e.activation(
                    xsq[:, 0:CH], xbb[:, d * CH:(d + 1) * CH], AF.Square),
                    reads=[xbname], writes=["xsq"], sem="s_act", tag=f"{tag}sq")
                P.op("pe", lambda e, d=d: e.matmul(
                    pD[0:1, 0:CH], onesb[:], xbb[:, d * CH:(d + 1) * CH],
                    start=(d == 0), stop=(d == ND - 1)),
                    reads=["onesb", xbname], writes=(["pD0"] if d in (0, ND - 1) else ()),
                    sem="s_pe", tag=f"{tag}sx")
                P.op("pe", lambda e, d=d: e.matmul(
                    pD[0:1, 512:512 + CH], onesb[:], xsq[:, 0:CH],
                    start=(d == 0), stop=(d == ND - 1)),
                    reads=["xsq"], writes=(["pD1"] if d in (0, ND - 1) else ()),
                    sem="s_pe", tag=f"{tag}sxx")
            P.op("dve", [lambda e: e.tensor_scalar(rowbuf[0:1, 0:CH], pD[0:1, 0:CH], 1.0 / D, None, ALU.mult),
                         lambda e: e.tensor_scalar(rowbuf[0:1, CH:2 * CH], pD[0:1, 512:512 + CH], 1.0 / D, None, ALU.mult),
                         None,
                         lambda e: e.tensor_tensor(f32a[0:1, 0:CH], rowbuf[0:1, 0:CH], rowbuf[0:1, 0:CH], ALU.mult),
                         None,
                         lambda e: e.tensor_tensor(rowbuf[0:1, CH:2 * CH], rowbuf[0:1, CH:2 * CH], f32a[0:1, 0:CH], ALU.subtract)],
                 reads=["pD0", "pD1"], writes=["mrow", "vrow", "f32a"], sem="s_dve", tag=f"{tag}mv")
            P.op("act", lambda e: e.activation(f32a[0:1, 0:CH], rowbuf[0:1, CH:2 * CH], AF.Sqrt,
                                               bias=cells[0:1, 0:1]),
                 reads=["vrow", "eps", "f32a"], writes=["sqv"], sem="s_act", tag=f"{tag}sqrt")
            P.op("dve", lambda e: e.reciprocal(rowbuf[0:1, CH:2 * CH], f32a[0:1, 0:CH]),
                 reads=["sqv"], writes=["rrow"], sem="s_dve", tag=f"{tag}r")
            P.op("dve", lambda e: e.tensor_tensor(xext[0:1, tok0:tok0 + CH], rowbuf[0:1, 0:CH],
                                                  rowbuf[0:1, CH:2 * CH], ALU.mult),
                 reads=["mrow", "rrow"], writes=["xext"], sem="s_dve", tag=f"{tag}mr")
            P.op("gp", lambda e: e.partition_broadcast(bcA[:, 0:CH], rowbuf[0:1, CH:2 * CH], channels=128),
                 reads=["rrow"], writes=["bcA"], sem="s_gp", tag=f"{tag}rbc")
            for d in range(ND):
                P.op("dve", lambda e, d=d: e.tensor_tensor(
                    lnl[:, d * CH:(d + 1) * CH], xb[:, d * CH:(d + 1) * CH],
                    bcA[:, 0:CH], ALU.mult),
                    reads=["bcA", xname], writes=[lnname, xbname], sem="s_dve", tag=f"{tag}lnr")

        def phase_qkv(b, wq):
            for k in range(T // CH):
                dma("sp", "q_x1s", xext[1:2, b * T + k * CH:b * T + (k + 1) * CH],
                    onesr[0:1, 0:CH], reads=["onesr"], writes=["xext"], tag="xext1")
            vg = vaug.take(P, 0, KT * 65 * H, "vaug")
            for tb in range(KT):
                P.op("dve", lambda e, tb=tb: e.memset(
                    vg[:, tb * 65 * H:(tb + 1) * 65 * H].rearrange(
                        "p (h x) -> p h x", x=65)[:, :, 64:65], 1.0),
                    writes=["vaug"], sem="s_dve", tag="v1")
            xc = [arF.take(P, 0, ND * CH, "x0"), arF.take(P, ND * CH, 2 * ND * CH, "x1")]
            xbc = [arB.take(P, 0, ND * CH, "xb0"), arB.take(P, ND * CH, 2 * ND * CH, "xb1")]
            lnc = xbc
            rope = [arR.take(P, i * CH, (i + 1) * CH, f"rp{i}") for i in range(8)]
            for c in range(NCH_B):
                sl = c % 2
                tok0 = b * T + c * CH
                for d in range(ND):
                    dma("sp", f"qx{sl}", xc[sl][:, d * CH:(d + 1) * CH],
                        xT_h[d * 128:(d + 1) * 128, tok0:tok0 + CH],
                        writes=[f"x{sl}"], tag="x ld")
                    dma("gp", f"gx{sl}", xbc[sl][:, d * CH:(d + 1) * CH],
                        xT_h[d * 128:(d + 1) * 128, tok0:tok0 + CH],
                        writes=[f"xb{sl}", f"ln{sl}"], tag="xb ld")
                ln_chunk(b, c, xc[sl], xbc[sl], lnc[sl], f"ln{sl}", f"x{sl}", f"xb{sl}",
                         f"L{b}.{c}")
                for f in range(2 * ND):
                    key, (ps, off) = pslot()
                    for d in range(ND):
                        P.op("pe", lambda e, f=f, d=d, ps=ps, off=off, sl=sl: e.matmul(
                            ps[:, off:off + CH],
                            wq[:, d * 3 * D + f * 128:d * 3 * D + f * 128 + 128],
                            lnc[sl][:, d * CH:(d + 1) * CH], start=(d == 0), stop=False),
                            reads=["wqbf", f"ln{sl}"], writes=([key] if d == 0 else ()),
                            sem="s_pe", tag="qkm")
                    P.op("pe", lambda e, f=f, ps=ps, off=off, tok0=tok0: e.matmul(
                        ps[:, off:off + CH], wext[:, f * 128:f * 128 + 128],
                        xext[:, tok0:tok0 + CH], start=False, stop=True),
                        reads=["wext", "xext"], writes=[key], sem="s_pe", tag="qke")
                    rs = f % 2
                    P.op("act", lambda e, ps=ps, off=off, rs=rs: e.activation(
                        rope[rs][:, 0:CH], ps[:, off:off + CH], AF.Copy),
                        reads=[key], writes=[f"rp{rs}"], sem="s_act", tag="rcp")
                    P.op("dve", lambda e, rs=rs: e.stream_shuffle(
                        rope[2 + rs][:, 0:CH], rope[rs][:, 0:CH], SHUF_MASK),
                        reads=[f"rp{rs}"], writes=[f"rp{2 + rs}"], sem="s_dve", tag="rsh")
                    P.op("dve", lambda e, rs=rs, c=c: e.tensor_tensor(
                        rope[4 + rs][:, 0:CH], rope[rs][:, 0:CH],
                        cosP[:, c * CH:c * CH + CH], ALU.mult),
                        reads=[f"rp{rs}", "cosP"], writes=[f"rp{4 + rs}"], sem="s_dve", tag="rc")
                    P.op("dve", lambda e, rs=rs, c=c: e.tensor_tensor(
                        rope[6 + rs][:, 0:CH], rope[2 + rs][:, 0:CH],
                        sinP[:, c * CH:c * CH + CH], ALU.mult),
                        reads=[f"rp{2 + rs}", "sinP"], writes=[f"rp{6 + rs}"], sem="s_dve", tag="rsm")
                    P.op("dve", lambda e, rs=rs: e.tensor_tensor(
                        rope[4 + rs][:, 0:CH], rope[4 + rs][:, 0:CH], rope[6 + rs][:, 0:CH],
                        ALU.add),
                        reads=[f"rp{6 + rs}"], writes=[f"rp{4 + rs}"], sem="s_dve", tag="ra")
                    is_q = f < ND
                    pair = f % ND
                    dst = qt_hbm if is_q else kt_hbm
                    dma("sp", f"qsp{b}", dst[b * PAIRS + pair, :, c * CH:c * CH + CH],
                        rope[4 + rs][:, 0:CH], reads=[f"rp{4 + rs}"],
                        writes=[f"qk_hbm{b}"], tag="qksp")
                for ts in range(CH // 128):
                    for vc in range(NVC):
                        key, (ps, off) = pslot()
                        for d in range(ND):
                            P.op("pe", lambda e, ts=ts, vc=vc, d=d, ps=ps, off=off, sl=sl: e.matmul(
                                ps[:, off:off + VW],
                                lnc[sl][:, d * CH + ts * 128:d * CH + ts * 128 + 128],
                                wq[:, d * 3 * D + 2 * D + vc * VW:d * 3 * D + 2 * D + vc * VW + VW],
                                start=(d == 0), stop=False),
                                reads=[f"ln{sl}", "wqbf"], writes=([key] if d == 0 else ()),
                                sem="s_pe", tag="vm")
                        P.op("pe", lambda e, ts=ts, vc=vc, ps=ps, off=off, tok0=tok0: e.matmul(
                            ps[:, off:off + VW],
                            xext[:, tok0 + ts * 128:tok0 + ts * 128 + 128],
                            wext[:, 2 * D + vc * VW:2 * D + vc * VW + VW],
                            start=False, stop=True),
                            reads=["xext", "wext"], writes=[key], sem="s_pe", tag="ve")
                        tb = c * (CH // 128) + ts
                        nh = VW // 64
                        P.op("dve", lambda e, ps=ps, off=off, tb=tb, vc=vc, nh=nh: e.tensor_copy(
                            vg[:, tb * 65 * H + vc * nh * 65:tb * 65 * H + (vc * nh + nh) * 65]
                            .rearrange("p (h x) -> p h x", x=65)[:, :, 0:64],
                            ps[:, off:off + VW]),
                            reads=[key], writes=["vaug"], sem="s_dve", tag="vev")

        def phase_attn(b):
            pta = arW.h[:, 0:KT * QCH]
            ptb = arW.h[:, KT * QCH:2 * KT * QCH]
            for kk in range(KT):
                arW.take(P, kk * QCH, (kk + 1) * QCH, f"ptA{kk}")
                arW.take(P, KT * QCH + kk * QCH, KT * QCH + (kk + 1) * QCH, f"ptB{kk}")
            qts = [arW.take(P, 2 * KT * QCH + i * T, 2 * KT * QCH + (i + 1) * T, f"qt{i}")
                   for i in range(2)]
            kts = [arW.take(P, 2 * KT * QCH + 2 * T + i * T,
                            2 * KT * QCH + 2 * T + (i + 1) * T, f"kt{i}")
                   for i in range(2)]
            vg = vaug.h[:, 0:KT * 65 * H]
            for p in range(PAIRS):
                sl = p % 2
                dma("sp", f"qk{sl}", qts[sl][:, 0:T], qt_hbm[b * PAIRS + p, :, :],
                    reads=[f"qk_hbm{b}"], writes=[f"qt{sl}"], tag="qtld")
                dma("sp", f"qk{sl}", kts[sl][:, 0:T], kt_hbm[b * PAIRS + p, :, :],
                    reads=[f"qk_hbm{b}"], writes=[f"kt{sl}"], tag="ktld")
                P.prod[f"qt{sl}"] = dict(P.prod[f"kt{sl}"])
                for qc in range(NQC):
                    q0 = qc * QCH
                    for kk in range(KT):
                        po = (kk % 2) * 512
                        pak, pbk = "pA%d" % (kk % 2), "pB%d" % (kk % 2)
                        P.op("pe", lambda e, kk=kk, q0=q0, sl=sl, po=po: e.matmul(
                            pA[:, po:po + QCH], kts[sl][0:64, kk * 128:kk * 128 + 128],
                            qts[sl][0:64, q0:q0 + QCH],
                            start=True, stop=True, tile_position=(0, 0)),
                            reads=[f"qt{sl}", f"kt{sl}"], writes=[pak],
                            sem="s_pe", tag="stA")
                        P.op("pe", lambda e, kk=kk, q0=q0, sl=sl, po=po: e.matmul(
                            pB[:, po:po + QCH], kts[sl][64:128, kk * 128:kk * 128 + 128],
                            qts[sl][64:128, q0:q0 + QCH],
                            start=True, stop=True, tile_position=(64, 0)),
                            reads=[f"qt{sl}", f"kt{sl}"], writes=[pbk],
                            sem="s_pe", tag="stB")
                        P.op("act", lambda e, kk=kk, po=po: e.activation(
                            pta[:, kk * QCH:kk * QCH + QCH], pA[:, po:po + QCH],
                            AF.Exp, scale=0.125),
                            reads=[pak], writes=[f"ptA{kk}"], sem="s_act", tag="expA")
                        P.op("act", lambda e, kk=kk, po=po: e.activation(
                            ptb[:, kk * QCH:kk * QCH + QCH], pB[:, po:po + QCH],
                            AF.Exp, scale=0.125),
                            reads=[pbk], writes=[f"ptB{kk}"], sem="s_act", tag="expB")
                    for kk in range(KT):
                        for head, (pt, psk) in enumerate(((pta, "pC0"), (ptb, "pD0"))):
                            ps, off = PS[psk]
                            hidx = 2 * p + head
                            P.op("pe", lambda e, kk=kk, pt=pt, ps=ps, off=off, hidx=hidx, q0=q0:
                                 e.matmul(ps[0:65, off:off + QCH],
                                          vg[:, kk * 65 * H + hidx * 65:kk * 65 * H + hidx * 65 + 65],
                                          pt[:, kk * QCH:kk * QCH + QCH],
                                          start=(kk == 0), stop=(kk == KT - 1)),
                                 reads=[f"pt{'AB'[head]}{kk}", "vaug"],
                                 writes=([psk] if kk in (0, KT - 1) else ()),
                                 sem="s_pe", tag="pv")
                    for head, psk in ((0, "pC0"), (1, "pD0")):
                        ps, off = PS[psk]
                        hidx = 2 * p + head
                        P.op("dve", [lambda e, ps=ps, off=off: e.tensor_copy(
                                        rowbuf[0:1, 2 * CH:2 * CH + QCH], ps[64:65, off:off + QCH]),
                                     None,
                                     lambda e: e.reciprocal(rowbuf[0:1, 2 * CH:2 * CH + QCH], rowbuf[0:1, 2 * CH:2 * CH + QCH])],
                             reads=[psk], writes=["lrow"], sem="s_dve", tag="linv")
                        P.op("gp", lambda e: e.partition_broadcast(bcB[0:64, 0:QCH],
                                                                   rowbuf[0:1, 2 * CH:2 * CH + QCH], channels=64),
                             reads=["lrow"], writes=["bcB"], sem="s_gp", tag="lbc")
                        stg = head * QCH
                        P.op("dve", lambda e, ps=ps, off=off, stg=stg: e.tensor_tensor(
                            austg[0:64, stg:stg + QCH], ps[0:64, off:off + QCH],
                            bcB[0:64, 0:QCH], ALU.mult),
                            reads=[psk, "bcB"], writes=[f"austg{head}"], sem="s_dve", tag="pvev")
                        dma("sp", "qau", au_hbm[hidx * 64:hidx * 64 + 64, q0:q0 + QCH],
                            austg[0:64, stg:stg + QCH], reads=[f"austg{head}"],
                            writes=["au_hbm"], tag="ausp")

        def quant_weight(nat_h, tr_h, qdst_h, srow_h, scols, ntn, wn, ntt, wt, tg):
            wna = arF.take(P, 0, 1408, f"{tg}wn")
            rcb = arF.take(P, 1408, 2816, f"{tg}rb")
            rowI = arF.take(P, 2816, 2816 + ntn * 128, f"{tg}rowI")
            wqo = arB.take(P, 0, 1408, f"{tg}wo")
            nhalf_n = (wn + 1407) // 1408
            for t in range(ntn):
                for hf in range(nhalf_n):
                    w0 = hf * 1408
                    w1 = min(wn, w0 + 1408)
                    dma("sp", "qwn", wna[:, 0:w1 - w0], nat_h[t * 128:(t + 1) * 128, w0:w1],
                        writes=[f"{tg}wn"], tag=f"{tg}nld")
                    P.op("dve", lambda e, w1=w1, w0=w0, hf=hf: e.reduce_max(
                        f32b[:, hf:hf + 1], wna[:, 0:w1 - w0], AXX,
                        apply_absolute_value=True),
                        reads=[f"{tg}wn"], writes=["f32bq"], sem="s_dve", tag=f"{tg}mx")
                P.op("dve", [lambda e, n=nhalf_n: e.reduce_max(f32b[:, 4:5], f32b[:, 0:n], AXX),
                             None,
                             lambda e, t=t: e.tensor_scalar(scols[:, t:t + 1], f32b[:, 4:5],
                                                            R127, None, ALU.mult)],
                     reads=["f32bq"], writes=[f"{tg}sc{t}", "f32bq"], sem="s_dve", tag=f"{tg}dv")
                dma("gp", "qsc", srow_h[0:1, t * 128:(t + 1) * 128], scols[:, t:t + 1],
                    reads=[f"{tg}sc{t}"], writes=[f"{tg}srow"], tag=f"{tg}spr")
            dma("sp", "qscl", rowI[0:1, 0:ntn * 128], srow_h[0:1, 0:ntn * 128],
                reads=[f"{tg}srow"], writes=[f"{tg}rowI"], tag=f"{tg}srl")
            P.op("dve", lambda e, n=ntn: e.reciprocal(rowI[0:1, 0:n * 128], rowI[0:1, 0:n * 128]),
                 reads=[f"{tg}rowI"], writes=[f"{tg}rowI"], sem="s_dve", tag=f"{tg}rc")
            nhalf_t = (wt + 1023) // 1024
            for hf in range(nhalf_t):
                w0 = hf * 1024
                w1 = min(wt, w0 + 1024)
                P.op("gp", lambda e, w0=w0, w1=w1: e.partition_broadcast(
                    rcb[:, 0:w1 - w0], rowI[0:1, w0:w1], channels=128),
                    reads=[f"{tg}rowI"], writes=[f"{tg}rb"], sem="s_gp", tag=f"{tg}rbc")
                for t in range(ntt):
                    dma("sp", "qwt", wna[:, 0:w1 - w0], tr_h[t * 128:(t + 1) * 128, w0:w1],
                        writes=[f"{tg}wn"], tag=f"{tg}tld")
                    P.op("dve", [lambda e, w0=w0, w1=w1: e.tensor_tensor(
                                    i32s[:, 0:w1 - w0], wna[:, 0:w1 - w0], rcb[:, 0:w1 - w0],
                                    ALU.mult),
                                 None,
                                 lambda e, w0=w0, w1=w1: e.tensor_scalar(
                                    wqo[:, 0:w1 - w0], i32s[:, 0:w1 - w0], -128.0, 127.0,
                                    ALU.max, ALU.min)],
                         reads=[f"{tg}wn", f"{tg}rb"], writes=[f"{tg}wo", "i32s"],
                         sem="s_dve", tag=f"{tg}q")
                    dma("sp", "qwqo", qdst_h[t * 128:(t + 1) * 128, w0:w1], wqo[:, 0:w1 - w0],
                        reads=[f"{tg}wo"], writes=[f"{tg}qhbm"], tag=f"{tg}qst")

        def phase_wout(b):
            xc = [arF.take(P, 0, ND * CH, "x0"), arF.take(P, ND * CH, 2 * ND * CH, "x1")]
            aus = [arB.take(P, 0, ND * CH, "au0")] * 2
            x1ch = arB.take(P, ND * CH, 2 * ND * CH, "x1ch")
            for c in range(NCH_B):
                sl = c % 2
                tok0 = b * T + c * CH
                for d in range(ND):
                    dma("sp", f"qx{sl}", xc[sl][:, d * CH:(d + 1) * CH],
                        xT_h[d * 128:(d + 1) * 128, tok0:tok0 + CH],
                        writes=[f"x{sl}"], tag="xre")
                    dma("sp", "qa0", aus[0][:, d * CH:(d + 1) * CH],
                        au_hbm[d * 128:(d + 1) * 128, c * CH:c * CH + CH],
                        reads=["au_hbm"], writes=["au0"], tag="aure")
                for ot in range(ND):
                    key, (ps, off) = pslot()
                    for d in range(ND):
                        P.op("pe", lambda e, ot=ot, d=d, ps=ps, off=off, sl=sl: e.matmul(
                            ps[:, off:off + CH],
                            woutT_bf[:, d * D + ot * 128:d * D + ot * 128 + 128],
                            aus[sl][:, d * CH:(d + 1) * CH], start=(d == 0), stop=False),
                            reads=["woutbf", "au0"], writes=([key] if d in (0, ND - 1) else ()), sem="s_pe", tag="wom")
                    P.op("pe", lambda e, ot=ot, ps=ps, off=off, tok0=tok0: e.matmul(
                        ps[:, off:off + CH], woext[:, ot * 128:ot * 128 + 128],
                        onesr[0:1, 0:CH], start=False, stop=True),
                        reads=["woext", "onesr"], writes=[key], sem="s_pe", tag="woe")
                    P.op("dve", lambda e, ot=ot, ps=ps, off=off, sl=sl: e.tensor_tensor(
                        x1ch[:, ot * CH:(ot + 1) * CH], ps[:, off:off + CH],
                        xc[sl][:, ot * CH:(ot + 1) * CH], ALU.add),
                        reads=[key, f"x{sl}"], writes=["x1ch"], sem="s_dve", tag="x1")
                    dma("sp", "qsx1", x1_hbm[ot * 128:(ot + 1) * 128, tok0:tok0 + CH],
                        x1ch[:, ot * CH:(ot + 1) * CH], reads=["x1ch"], writes=["x13hbm"],
                        tag="x1sp")
                for d in range(ND):
                    P.op("act", lambda e, d=d: e.activation(
                        xsq[:, 0:CH], x1ch[:, d * CH:(d + 1) * CH], AF.Square),
                        reads=["x1ch"], writes=["xsq"], sem="s_act", tag="l2sq")
                    P.op("pe", lambda e, d=d: e.matmul(
                        pD[0:1, 0:CH], onesb[:], x1ch[:, d * CH:(d + 1) * CH],
                        start=(d == 0), stop=(d == ND - 1)),
                        reads=["onesb", "x1ch"], writes=(["pD0"] if d in (0, ND - 1) else ()),
                        sem="s_pe", tag="l2sx")
                    P.op("pe", lambda e, d=d: e.matmul(
                        pD[0:1, 512:512 + CH], onesb[:], xsq[:, 0:CH],
                        start=(d == 0), stop=(d == ND - 1)),
                        reads=["xsq"], writes=(["pD1"] if d in (0, ND - 1) else ()),
                        sem="s_pe", tag="l2sxx")
                P.op("dve", [lambda e: e.tensor_scalar(rowbuf[0:1, 0:CH], pD[0:1, 0:CH], 1.0 / D, None, ALU.mult),
                             lambda e: e.tensor_scalar(rowbuf[0:1, CH:2 * CH], pD[0:1, 512:512 + CH], 1.0 / D, None, ALU.mult),
                             None,
                             lambda e: e.tensor_tensor(f32a[0:1, 0:CH], rowbuf[0:1, 0:CH], rowbuf[0:1, 0:CH], ALU.mult),
                             None,
                             lambda e: e.tensor_tensor(rowbuf[0:1, CH:2 * CH], rowbuf[0:1, CH:2 * CH], f32a[0:1, 0:CH], ALU.subtract)],
                     reads=["pD0", "pD1"], writes=["mrow", "vrow", "f32a"], sem="s_dve", tag="l2mv")
                P.op("act", lambda e: e.activation(f32a[0:1, 0:CH], rowbuf[0:1, CH:2 * CH], AF.Sqrt,
                                                   bias=cells[0:1, 0:1]),
                     reads=["vrow", "eps", "f32a"], writes=["sqv"], sem="s_act", tag="l2sqrt")
                P.op("dve", lambda e: e.reciprocal(rowbuf[0:1, CH:2 * CH], f32a[0:1, 0:CH]),
                     reads=["sqv"], writes=["rrow"], sem="s_dve", tag="l2r")
                P.op("gp", lambda e: e.partition_broadcast(bcA[:, 0:CH], rowbuf[0:1, CH:2 * CH], channels=128),
                     reads=["rrow"], writes=["bcA"], sem="s_gp", tag="l2rbc")
                P.op("gp", lambda e: e.partition_broadcast(bcB[:, 0:CH], rowbuf[0:1, 0:CH], channels=128),
                     reads=["mrow"], writes=["bcB"], sem="s_gp", tag="l2mbc")
                for d in range(ND):
                    mcx = (b * NCH_B + c) * ND + d
                    P.op("dve", [lambda e, d=d: e.tensor_tensor(
                                    f32b[:, 0:CH], x1ch[:, d * CH:(d + 1) * CH], bcB[:, 0:CH],
                                    ALU.subtract),
                                 None,
                                 lambda e: e.tensor_tensor(ln2t[:, 0:CH], f32b[:, 0:CH],
                                                           bcA[:, 0:CH], ALU.mult),
                                 None,
                                 lambda e, mcx=mcx: e.reduce_max(maxcols[:, mcx:mcx + 1],
                                                                 ln2t[:, 0:CH], AXX,
                                                                 apply_absolute_value=True)],
                         reads=["x1ch", "bcA", "bcB"], writes=["ln2t", "f32b", "maxcols"],
                         sem="s_dve", tag="ln2")
                    dma("sp", "qsx1", ln2_hbm[d * 128:(d + 1) * 128, tok0:tok0 + CH],
                        ln2t[:, 0:CH], reads=["ln2t"], writes=["x13hbm"], tag="l2sp")

        def allreduce_max(srcname, src_ap, cci, cco, clipname, clip_ap, scale_col,
                          recip_col, tg):
            P.op("dve", [lambda e: e.memset(scr1[0:1, 0:8], 0.0),
                         None,
                         lambda e: e.reduce_max(f32a[:, 0:1], src_ap, AXX)],
                 reads=[srcname], writes=["f32a", "scr1"], sem="s_dve", tag=tg + "rd")
            P.op("gp", lambda e: e.partition_all_reduce(f32a[:, 0:1], f32a[:, 0:1],
                                                        channels=128, reduce_op=ReduceOp.max),
                 reads=["f32a"], writes=["f32a"], sem="s_gp", tag=tg + "pr")
            P.op("dve", lambda e: e.tensor_copy(scr1[0:1, 0:1], f32a[0:1, 0:1]),
                 reads=["f32a"], writes=["scr1"], sem="s_dve", tag=tg + "cp")
            dma("gp", "qcc", cci[:, :], scr1[0:1, 0:8], reads=["scr1"], tag=tg + "cw")
            if cfg.get("no_cc"):
                P.op("gp", lambda e: e.dma_start(out=cco[:, :], in_=cci[:, :]),
                     waits=[("qcc", P.mark("qcc"))], sem="s_cc", dk=16, tag=tg + "cc")
                P.op("gp", lambda e: e.nop(), waits=[("s_cc", P.mark("s_cc"))],
                     sem="s_ccn", tag=tg + "ccw")
            else:
                P.op("gp", lambda e: e.collective_compute(
                    "AllReduce", ALU.max, replica_groups=[list(range(NCORES))],
                    ins=[cci[:, :].opt()], outs=[cco[:, :].opt()]),
                    waits=[("qcc", P.mark("qcc"))], sem="s_cc", tag=tg + "cc")
            dma("gp", "qcc", scr1[0:1, 0:8], cco[:, :],
                waits=[("s_cc", P.mark("s_cc"))], writes=["scr1"], tag=tg + "cr")
            P.op("act", lambda e: e.activation(cells[0:1, 11:12], clip_ap, AF.Sigmoid),
                 reads=[clipname], writes=["sig"], sem="s_act", tag=tg + "sg")
            P.op("dve", [lambda e: e.tensor_tensor(cells[0:1, 13:14], scr1[0:1, 0:1],
                                                   cells[0:1, 11:12], ALU.mult),
                         None,
                         lambda e: e.tensor_scalar(cells[0:1, 14:15], cells[0:1, 13:14],
                                                   R127, None, ALU.mult),
                         None,
                         lambda e: e.reciprocal(cells[0:1, 15:16], cells[0:1, 14:15])],
                 reads=["scr1", "sig"], writes=[tg + "sv"], sem="s_dve", tag=tg + "sc")
            P.op("gp", lambda e, sc=scale_col: e.partition_broadcast(
                cells[:, sc:sc + 1], cells[0:1, 14:15], channels=128),
                reads=[tg + "sv"], writes=[tg + "scb"], sem="s_gp", tag=tg + "sb")
            P.op("gp", lambda e, rc=recip_col: e.partition_broadcast(
                cells[:, rc:rc + 1], cells[0:1, 15:16], channels=128),
                reads=[tg + "sv"], writes=[tg + "rcb"], sem="s_gp", tag=tg + "rb")

        def phase_xq():
            xqb = arW.take(P, 0, ND * TOK, "xq")
            l2r = arB.take(P, 0, TOK, "ln2rd")
            HQ = min(TOK, 1024)
            for d in range(ND):
                dma("sp", "qln", l2r[:, 0:TOK], ln2_hbm[d * 128:(d + 1) * 128, 0:TOK],
                    reads=["x13hbm"], writes=["ln2rd"], tag="l2re")
                for hh in range(TOK // HQ):
                    P.op("dve", [lambda e, hh=hh: e.tensor_scalar(
                                    i32s[:, 0:HQ], l2r[:, hh * HQ:(hh + 1) * HQ],
                                    cells[:, 1:2], None, ALU.mult),
                                 None,
                                 lambda e, d=d, hh=hh: e.tensor_scalar(
                                    xqb[:, d * TOK + hh * HQ:d * TOK + (hh + 1) * HQ],
                                    i32s[:, 0:HQ], -128.0, 127.0, ALU.max, ALU.min)],
                         reads=["ln2rd", "m1rcb"], writes=["xq", "i32s"],
                         sem="s_dve", tag="xq")

        def phase_gateup():
            NCH = TOK // 512
            sgl = arF.take(P, 0, 512, "sgl")
            gsl = [arR.take(P, i * 1024, (i + 1) * 1024, f"wgs{i}") for i in range(2)]
            usl = [arR.take(P, 2048 + i * 1024, 2048 + (i + 1) * 1024, f"wus{i}")
                   for i in range(2)]
            P.op("dve", lambda e: e.tensor_scalar(sgq[:, 0:NI], sgcols[:, 0:NI],
                                                  cells[:, 3:4], None, ALU.mult),
                 reads=["m1scb"] + [f"gsc{t}" for t in range(NI)], writes=["sgq"],
                 sem="s_dve", tag="sgq")
            P.op("dve", lambda e: e.tensor_scalar(suq[:, 0:NI], sucols[:, 0:NI],
                                                  cells[:, 3:4], None, ALU.mult),
                 reads=["m1scb"] + [f"usc{t}" for t in range(NI)], writes=["suq"],
                 sem="s_dve", tag="suq")
            xqb = arW.h[:, 0:ND * TOK]
            for it in range(NI):
                sl = it % 2
                for d in range(ND):
                    dma("sp", f"qgl{sl}", gsl[sl][:, d * 128:(d + 1) * 128],
                        wgq_hbm[d * 128:(d + 1) * 128, it * 128:(it + 1) * 128],
                        reads=["gqhbm"], writes=[f"wgs{sl}"], tag="gld")
                    dma("sp", f"qul{sl}", usl[sl][:, d * 128:(d + 1) * 128],
                        wuq_hbm[d * 128:(d + 1) * 128, it * 128:(it + 1) * 128],
                        reads=["uqhbm"], writes=[f"wus{sl}"], tag="uld")
                for ch in range(NCH):
                    kg, (psg, offg) = pslot()
                    ku, (psu, offu) = pslot()
                    for d in range(ND):
                        P.op("pe", lambda e, d=d, ch=ch, psg=psg, offg=offg, sl=sl: e.matmul(
                            psg[:, offg:offg + 512], gsl[sl][:, d * 128:(d + 1) * 128],
                            xqb[:, d * TOK + ch * 512:d * TOK + ch * 512 + 512],
                            start=(d == 0), stop=(d == ND - 1)),
                            reads=[f"wgs{sl}", "xq"], writes=([kg] if d in (0, ND - 1) else ()),
                            sem="s_pe", tag="gm")
                    for d in range(ND):
                        P.op("pe", lambda e, d=d, ch=ch, psu=psu, offu=offu, sl=sl: e.matmul(
                            psu[:, offu:offu + 512], usl[sl][:, d * 128:(d + 1) * 128],
                            xqb[:, d * TOK + ch * 512:d * TOK + ch * 512 + 512],
                            start=(d == 0), stop=(d == ND - 1)),
                            reads=[f"wus{sl}", "xq"], writes=([ku] if d in (0, ND - 1) else ()),
                            sem="s_pe", tag="um")
                    P.op("act", lambda e, it=it, psg=psg, offg=offg: e.activation(
                        sgl[:, 0:512], psg[:, offg:offg + 512], AF.Silu,
                        scale=sgq[:, it:it + 1]),
                        reads=[kg, "sgq"], writes=["sgl"], sem="s_act", tag="silu")
                    P.op("dve", lambda e, it=it, psu=psu, offu=offu: e.scalar_tensor_tensor(
                        hbf[:, 0:512], psu[:, offu:offu + 512], suq[:, it:it + 1],
                        sgl[:, 0:512], ALU.mult, ALU.mult),
                        reads=[ku, "sgl", "suq"], writes=["hbf"], sem="s_dve", tag="h")
                    P.op("dve", lambda e, it=it, ch=ch: e.reduce_max(
                        hmx[:, it * NCH + ch:it * NCH + ch + 1], hbf[:, 0:512], AXX,
                        apply_absolute_value=True),
                        reads=["hbf"], writes=["hmx"], sem="s_dve", tag="hmax")
                    dma("sp", "qh", h_hbm[it * 128:(it + 1) * 128, ch * 512:ch * 512 + 512],
                        hbf[:, 0:512], reads=["hbf"], writes=["hhbm"], tag="hsp")

        def phase_down():
            wda = arW.take(P, 0, NI * D, "wda")
            of32 = arF.take(P, 1536, 2048, "of32")
            x1rb = arB.take(P, ABN - ND * 512, ABN, "x1rb")
            for t in range(NI):
                dma("sp", "qwd", wda[:, t * D:(t + 1) * D],
                    wdq_hbm[t * 128:(t + 1) * 128, 0:D], reads=["dqhbm"], writes=["wda"],
                    tag="wdld")
            P.op("dve", lambda e: e.tensor_scalar(sdq[:, 0:ND], sdcols[:, 0:ND],
                                                  cells[:, 4:5], None, ALU.mult),
                 reads=["m2scb"] + [f"dsc{t}" for t in range(ND)], writes=["sdq"],
                 sem="s_dve", tag="sdq")
            NC4 = TOK // 512
            for c4 in range(NC4):
                for d in range(ND):
                    dma("sp", "qx1r", x1rb[:, d * 512:(d + 1) * 512],
                        x1_hbm[d * 128:(d + 1) * 128, c4 * 512:c4 * 512 + 512],
                        reads=["x13hbm"], writes=["x1rb"],
                        waits=[P.cons.get("x1rb")], tag="x1re")
                for it in range(NI):
                    sl = it % 2
                    dma("sp", f"qhl{sl}", hrs[sl][:, 0:512],
                        h_hbm[it * 128:(it + 1) * 128, c4 * 512:c4 * 512 + 512],
                        reads=["hhbm"], writes=[f"hr{sl}"], tag="hre")
                    P.op("dve", [lambda e, sl=sl: e.tensor_scalar(
                                    i32s[:, 0:512], hrs[sl][:, 0:512], cells[:, 2:3],
                                    None, ALU.mult),
                                 None,
                                 lambda e, sl=sl: e.tensor_scalar(hqt[sl][:, 0:512],
                                                                  i32s[:, 0:512],
                                                                  -128.0, 127.0,
                                                                  ALU.max, ALU.min)],
                         reads=[f"hr{sl}", "m2rcb"], writes=[f"hqt{sl}", "i32s"],
                         sem="s_dve", tag="hq")
                    for ot in range(ND):
                        psk = PSALL[ot]
                        ps, off = PS[psk]
                        P.op("pe", lambda e, it=it, ot=ot, ps=ps, off=off, sl=sl: e.matmul(
                            ps[:, off:off + 512], wda[:, it * D + ot * 128:it * D + ot * 128 + 128],
                            hqt[sl][:, 0:512], start=(it == 0), stop=(it == NI - 1)),
                            reads=["wda", f"hqt{sl}"], writes=([psk] if it in (0, NI - 1) else ()),
                            sem="s_pe", tag="dnm")
                for ot in range(ND):
                    psk = PSALL[ot]
                    ps, off = PS[psk]
                    P.op("dve", lambda e, ot=ot, ps=ps, off=off: e.scalar_tensor_tensor(
                        of32[:, 0:512], ps[:, off:off + 512], sdq[:, ot:ot + 1],
                        x1rb[:, ot * 512:(ot + 1) * 512], ALU.mult, ALU.add),
                        reads=[psk, "sdq", "x1rb"], writes=["of32"], sem="s_dve", tag="dne")
                    dma("sp", "qout", outT_h[ot * 128:(ot + 1) * 128, c4 * 512:c4 * 512 + 512],
                        of32[:, 0:512], reads=["of32"], writes=["outhbm"], tag="ost")

        stop = cfg.get("stop_after", "")
        def _done():
            P.op("sp", lambda e: e.nop(),
                 waits=[(sname, P.mark(sname)) for sname in list(P.C)],
                 sem="s_fin", tag="final")
        while True:
            if stop == "p0":
                break
            phase_qkv(0, wqbf)
            if stop == "qkv":
                break
            phase_attn(0)
            if stop == "attn":
                break
            quant_weight(wg_h, wgT_h, wgq_hbm, sg_hbm, sgcols, NI, D, ND, I, "g")
            quant_weight(wu_h, wuT_h, wuq_hbm, su_hbm, sucols, NI, D, ND, I, "u")
            quant_weight(wd_h, wdT_h, wdq_hbm, sd_hbm, sdcols, ND, I, NI, D, "d")
            if stop == "quant":
                break
            phase_wout(0)
            if stop == "wout":
                break
            if NB > 1:
                wqbf2 = load_wqkv()
                phase_qkv(1, wqbf2)
                phase_attn(1)
                phase_wout(1)
            allreduce_max("maxcols", maxcols[:, 0:ND * NCH_B * NB], cc1i, cc1o,
                          "rows0", cells[0:1, 12:13], 3, 1, "m1")
            if stop == "cc1":
                break
            phase_xq()
            if stop == "xq":
                break
            phase_gateup()
            if stop == "gateup":
                break
            allreduce_max("hmx", hmx[:, 0:NI * (TOK // 512)], cc2i, cc2o,
                          "rows0", cells[0:1, 5:6], 4, 2, "m2")
            if stop == "cc2":
                break
            phase_down()
            break
        _done()

        semobjs = {s: es.enter_context(nc.semaphore(s)) for s in P.C}

        def emitter(engname):
            def run(eng):
                hw = {}
                for waits, fns, inc, tag in P.ops[engname]:
                    for s, v in waits:
                        if hw.get(s, 0) < v:
                            eng.wait_ge(semobjs[s], v)
                            hw[s] = v
                    last = None
                    for fn in fns:
                        if fn is None:
                            eng.drain()
                        else:
                            last = fn(eng)
                    s, k = inc
                    assert last is not None, f"no instruction in op: {tag}"
                    last.then_inc(semobjs[s], k)
            return run

        with nc.Block() as block:
            block.sync(emitter("sp"))
            block.scalar(emitter("act"))
            block.vector(emitter("dve"))
            block.tensor(emitter("pe"))
            block.gpsimd(emitter("gp"))

    mybir.codegen_inst_isa_subclasses(nc)
    return nc


def _qk_perm(D, H):
    n = 2 * D
    hd = D // H
    half = hd // 2
    p = np.zeros(n, dtype=np.int64)
    for sec in range(2):
        for h in range(H):
            base = sec * D + h * hd
            for j in range(half):
                p[base + 2 * j] = base + j
                p[base + 2 * j + 1] = base + half + j
    return p


def make_inputs_for_core(core, x, Wqkv, bqkv, Wout, bout, clip_x, clip_down,
                         wg, wu, wd, NB):
    D = x.shape[2]
    H = 16
    xs = x[core * NB:(core + 1) * NB]
    xT = np.ascontiguousarray(np.concatenate([s.T for s in xs], axis=1))
    perm = _qk_perm(D, H)
    Wq = np.array(Wqkv)
    Wq[:2 * D] = Wqkv[perm]
    bq = np.array(bqkv)
    bq[:2 * D] = bqkv[perm]
    f32 = np.float32
    return {
        "xT": xT.astype(f32),
        "wqkvT": np.ascontiguousarray(Wq.T).astype(f32),
        "bqkv": bq.reshape(1, -1).astype(f32),
        "woutT": np.ascontiguousarray(np.array(Wout).T).astype(f32),
        "bout": np.array(bout).reshape(1, -1).astype(f32),
        "clip_x": np.array(clip_x).reshape(1, 1).astype(f32),
        "clip_down": np.array(clip_down).reshape(1, 1).astype(f32),
        "wg": np.ascontiguousarray(wg).astype(f32),
        "wgT": np.ascontiguousarray(np.array(wg).T).astype(f32),
        "wu": np.ascontiguousarray(wu).astype(f32),
        "wuT": np.ascontiguousarray(np.array(wu).T).astype(f32),
        "wd": np.ascontiguousarray(wd).astype(f32),
        "wdT": np.ascontiguousarray(np.array(wd).T).astype(f32),
    }


def kernel(L0_attn_x, L0_attn_Wqkv, L0_attn_bqkv, L0_attn_Wout, L0_attn_bout,
           L0_ffn_clip_x, L0_ffn_clip_down, L0_ffn_w_gate, L0_ffn_w_up, L0_ffn_w_down):
    x = np.asarray(L0_attn_x)
    B, T, D = x.shape
    NCORES = 8
    NB = B // NCORES
    cfg = dict(T=T, D=D, H=16, I=int(np.asarray(L0_ffn_w_gate).shape[0]), NB=NB,
               ncores=NCORES)
    nc = build(cfg)
    args = (x, np.asarray(L0_attn_Wqkv), np.asarray(L0_attn_bqkv),
            np.asarray(L0_attn_Wout), np.asarray(L0_attn_bout),
            np.asarray(L0_ffn_clip_x), np.asarray(L0_ffn_clip_down),
            np.asarray(L0_ffn_w_gate), np.asarray(L0_ffn_w_up),
            np.asarray(L0_ffn_w_down))
    in_maps = [make_inputs_for_core(c, *args, NB) for c in range(NCORES)]
    res = run_bass_kernel_spmd(nc, in_maps, core_ids=list(range(NCORES)))
    out = np.empty((B, T, D), dtype=np.float32)
    for c in range(NCORES):
        oT = np.asarray(res.results[c]["outT"])
        for b in range(NB):
            out[c * NB + b] = oT[:, b * T:(b + 1) * T].T
    return out
